# version 04
# baseline (speedup 1.0000x reference)
import numpy as np
from contextlib import ExitStack

SERIAL_SECTS = set()

import concourse.bass as bass
import concourse.mybir as mybir
from concourse.bass_utils import run_bass_kernel_spmd

F32 = mybir.dt.float32
BF16 = mybir.dt.bfloat16
I32 = mybir.dt.int32
AF = mybir.ActivationFunctionType
ALU = mybir.AluOpType
AX = mybir.AxisListType

NCORE = 8
SEQS = 2
S = 2048
C = 1024
H = 16
D = 64
DH = 4096
NCH = C // 128
NHC = DH // 128
TT = 512
NTT = S // TT
KC = S // 128
QBOUND = 8.06 * 0.125 * 1.002


class Sched:
    ENGS = ("tensor", "vector", "scalar", "gpsimd", "sync")
    SEM_OF = {"tensor": "pe", "vector": "dve", "scalar": "act",
              "gpsimd": "pl", "sync": "in"}

    def __init__(self, nc):
        self.nc = nc
        self.sems = {}
        self.cnt = {k: 0 for k in ("pe", "dve", "act", "pl", "in")}
        self.q = {e: [] for e in self.ENGS}
        self.hw = {e: {} for e in self.ENGS}
        self.serial = False

    AUTO_CHAIN = {"vector", "scalar", "gpsimd"}

    def op(self, eng, meth, *args, waits=(), inc=1, **kw):
        sem = self.SEM_OF[eng]
        if self.serial:
            waits = list(waits) + [(s_, self.cnt[s_]) for s_ in self.cnt]
        elif eng in self.AUTO_CHAIN and meth != "dma_start":
            waits = list(waits) + [(sem, self.cnt[sem])]
        ws = []
        flat = []
        for wv in waits:
            if isinstance(wv, list):
                flat.extend(wv)
            else:
                flat.append(wv)
        for wv in flat:
            if wv is None:
                continue
            s_, v = wv
            if v > self.hw[eng].get(s_, 0):
                self.hw[eng][s_] = v
                ws.append((s_, v))
        self.q[eng].append((ws, meth, args, kw, sem if inc else None, inc))
        if inc:
            self.cnt[sem] += inc
        return (sem, self.cnt[sem])

    def ev(self, eng):
        return (self.SEM_OF[eng], self.cnt[self.SEM_OF[eng]])

    def vop(self, meth, *args, waits=(), **kw):
        return self.op("vector", meth, *args,
                       waits=list(waits) + [self.ev("vector")], **kw)

    def flush(self, blk):
        for eng in self.ENGS:
            entries = self.q[eng]
            if not entries:
                continue
            sems = self.sems

            def body(E, entries=entries):
                for ws, meth, args, kw, sem, inc in entries:
                    for (s_, v) in ws:
                        E.wait_ge(sems[s_], v)
                    if meth == "matmul":
                        kw.setdefault("skip_group_check", True)
                    if meth == "load_act_set":
                        inst = E.add_instruction(mybir.InstLoadActFuncSet(
                            name=E.bass.get_next_instruction_name(),
                            act_func_set_id=args[0], ins=[], outs=[]))
                    else:
                        inst = getattr(E, meth)(*args, **kw)
                    if sem is not None:
                        inst.then_inc(sems[sem], inc)
            getattr(blk, eng)(body)
            self.q[eng] = []


def build(nc, debug=0):
    x_in = nc.dram_tensor("x", [SEQS, C, S], F32, kind="ExternalInput").ap()
    wqkvT = nc.dram_tensor("wqkvT", [C, 3 * C], F32, kind="ExternalInput").ap()
    wprojT = nc.dram_tensor("wprojT", [C, C], F32, kind="ExternalInput").ap()
    w1T = nc.dram_tensor("w1T", [C, DH], F32, kind="ExternalInput").ap()
    w2T = nc.dram_tensor("w2T", [DH, C], F32, kind="ExternalInput").ap()
    bqkv_col = nc.dram_tensor("bqkv_col", [128, 24], F32, kind="ExternalInput").ap()
    bv_row = nc.dram_tensor("bv_row", [1, C], F32, kind="ExternalInput").ap()
    bproj_col = nc.dram_tensor("bproj_col", [128, 8], F32, kind="ExternalInput").ap()
    convw_col = nc.dram_tensor("convw_col", [128, 3 * NHC], F32, kind="ExternalInput").ap()
    convb_col = nc.dram_tensor("convb_col", [128, NHC], F32, kind="ExternalInput").ap()
    qn2_in = nc.dram_tensor("qn2", [128, 1], F32, kind="ExternalInput").ap()
    kn2_in = nc.dram_tensor("kn2", [128, 1], F32, kind="ExternalInput").ap()
    ident_in = nc.dram_tensor("ident", [128, 128], F32, kind="ExternalInput").ap()
    sel2_in = nc.dram_tensor("sel2", [2, 128], F32, kind="ExternalInput").ap()
    blkones_in = nc.dram_tensor("blkones", [128, 2], F32, kind="ExternalInput").ap()

    out = nc.dram_tensor("out", [SEQS, C, S], F32, kind="ExternalOutput").ap()

    wq_kind = "ExternalOutput" if debug >= 2 else "Internal"
    w1q_s = nc.dram_tensor("w1q_s", [NCH, 128, DH], BF16, kind=wq_kind).ap()
    w2q_s = nc.dram_tensor("w2q_s", [NHC, 128, C], BF16, kind=wq_kind).ap()
    xp_kind = "ExternalOutput" if debug else "Internal"
    xp_s = nc.dram_tensor("xp_s", [SEQS, NCH, 128, S], F32, kind=xp_kind).ap()
    dbg = None
    if debug >= 2:
        dbg = {
            "xq2": nc.dram_tensor("dbg_xq2", [128, NCH * TT], BF16, kind="ExternalOutput").ap(),
            "stt": nc.dram_tensor("dbg_stt", [128, NHC * TT], F32, kind="ExternalOutput").ap(),
            "abc": nc.dram_tensor("dbg_abc", [128, 3 * TT], F32, kind="ExternalOutput").ap(),
            "a2r": nc.dram_tensor("dbg_a2r", [128, TT], F32, kind="ExternalOutput").ap(),
            "d2r": nc.dram_tensor("dbg_d2r", [128, TT], F32, kind="ExternalOutput").ap(),
            "hq0": nc.dram_tensor("dbg_hq0", [128, TT], BF16, kind="ExternalOutput").ap(),
            "tmr": nc.dram_tensor("dbg_tmr", [8, 128, 16], F32, kind="ExternalOutput").ap(),
            "hqa": nc.dram_tensor("dbg_hqa", [NTT, NHC, 128, TT], BF16, kind="ExternalOutput").ap(),
            "stta": nc.dram_tensor("dbg_stta", [NTT, 128, NHC * TT], F32, kind="ExternalOutput").ap(),
            "xq2a": nc.dram_tensor("dbg_xq2a", [NTT, 128, NCH * TT], BF16, kind="ExternalOutput").ap(),
            "a2a": nc.dram_tensor("dbg_a2a", [NTT, 128, TT], F32, kind="ExternalOutput").ap(),
            "d2a": nc.dram_tensor("dbg_d2a", [NTT, 128, TT], F32, kind="ExternalOutput").ap(),
            "xra": nc.dram_tensor("dbg_xra", [NTT, 8, 128, TT], F32, kind="ExternalOutput").ap(),
        }

    st = ExitStack()
    sch = Sched(nc)
    for nm in ("pe", "dve", "act", "pl", "in"):
        sch.sems[nm] = st.enter_context(nc.semaphore(nm))

    IDF = st.enter_context(nc.sbuf_tensor("IDF", [128, 128], F32))
    IDB = st.enter_context(nc.sbuf_tensor("IDB", [128, 128], BF16))
    SEL2 = st.enter_context(nc.sbuf_tensor("SEL2", [2, 128], BF16))
    BLK1 = st.enter_context(nc.sbuf_tensor("BLK1", [128, 2], BF16))
    ONE_C_B = st.enter_context(nc.sbuf_tensor("ONE_C_B", [128, 1], BF16))
    ONE_R_B = st.enter_context(nc.sbuf_tensor("ONE_R_B", [1, 128], BF16))
    ONE_C_F = st.enter_context(nc.sbuf_tensor("ONE_C_F", [128, 1], F32))
    ONE_R_F = st.enter_context(nc.sbuf_tensor("ONE_R_F", [1, 128], F32))
    BQKV = st.enter_context(nc.sbuf_tensor("BQKV", [128, 24], F32))
    BPROJ = st.enter_context(nc.sbuf_tensor("BPROJ", [128, 8], F32))
    CONVW = st.enter_context(nc.sbuf_tensor("CONVW", [128, 3 * NHC], F32))
    CONVB = st.enter_context(nc.sbuf_tensor("CONVB", [128, NHC], F32))
    GCOL = st.enter_context(nc.sbuf_tensor("GCOL", [128, 1], F32))
    VBIAS = st.enter_context(nc.sbuf_tensor("VBIAS", [128, C], F32))
    INVW1 = st.enter_context(nc.sbuf_tensor("INVW1", [128, 1], F32))
    INVW2 = st.enter_context(nc.sbuf_tensor("INVW2", [128, 1], F32))
    EPSA = st.enter_context(nc.sbuf_tensor("EPSA", [128, 2], F32))

    with nc.Block() as blk:
        with ExitStack() as ls:
            QN2 = ls.enter_context(nc.sbuf_tensor("QN2", [128, 1], F32))
            KN2 = ls.enter_context(nc.sbuf_tensor("KN2", [128, 1], F32))
            SELF_ = ls.enter_context(nc.sbuf_tensor("SELF", [2, 128], F32))
            BLKF = ls.enter_context(nc.sbuf_tensor("BLKF", [128, 2], F32))
            BVH = ls.enter_context(nc.sbuf_tensor("BVH", [1, C], BF16))
            BVL = ls.enter_context(nc.sbuf_tensor("BVL", [1, C], BF16))
            BVR = ls.enter_context(nc.sbuf_tensor("BVR", [1, C], F32))
            PB = ls.enter_context(nc.psum_tensor("PBCS", [128, 512], F32))

            for (dst, src) in ((IDF, ident_in), (SELF_, sel2_in), (BLKF, blkones_in),
                               (BQKV, bqkv_col), (BPROJ, bproj_col),
                               (CONVW, convw_col), (CONVB, convb_col),
                               (QN2, qn2_in), (KN2, kn2_in), (BVR, bv_row)):
                sch.op("sync", "dma_start", dst[:, :], src, inc=16)
            ein = sch.ev("sync")
            sch.op("vector", "tensor_copy", IDB[:, :], IDF[:, :], waits=[ein])
            sch.op("vector", "tensor_copy", SEL2[:, :], SELF_[:, :])
            sch.op("vector", "tensor_copy", BLK1[:, :], BLKF[:, :])
            sch.op("vector", "memset", ONE_C_B[:, :], 1.0)
            sch.op("vector", "memset", ONE_R_B[:, :], 1.0)
            sch.op("vector", "memset", ONE_C_F[:, :], 1.0)
            sch.op("vector", "memset", ONE_R_F[:, :], 1.0)
            sch.op("vector", "memset", EPSA[:, 0:1], 1e-5)
            sch.op("vector", "memset", EPSA[:, 1:2], 1e-6)
            sch.op("vector", "tensor_mul", GCOL[:, :], QN2[:, :], KN2[:, :])
            sch.op("vector", "tensor_copy", BVH[:, :], BVR[:, :])
            sch.op("vector", "tensor_sub", BVL[:, :], BVR[:, :], BVH[:, :])
            edve = sch.ev("vector")
            eact = None
            for half in range(2):
                sch.op("tensor", "matmul", PB[:, :], ONE_R_B[:, :],
                       BVH[0:1, half * 512:(half + 1) * 512],
                       start=True, stop=False, waits=[edve, eact])
                sch.op("tensor", "matmul", PB[:, :], ONE_R_B[:, :],
                       BVL[0:1, half * 512:(half + 1) * 512],
                       start=False, stop=True)
                epe = sch.ev("tensor")
                eact = sch.op("scalar", "activation",
                              VBIAS[:, half * 512:(half + 1) * 512], PB[:, :],
                              AF.Identity, bias=0.0, scale=1.0, waits=[epe])
            sch.flush(blk)

    for (wsrc, wdst, nch, wd, invcol, tag) in (
            (w1T, w1q_s, NCH, DH, INVW1, "w1"),
            (w2T, w2q_s, NHC, C, INVW2, "w2")):
        with nc.Block() as blk:
            with ExitStack() as ls:
                WF = ls.enter_context(nc.sbuf_tensor(f"WF{tag}", [128, nch * wd], F32))
                SA = ls.enter_context(nc.sbuf_tensor(f"SA{tag}", [128, nch], F32))
                QS = ls.enter_context(nc.sbuf_tensor(f"QS{tag}", [128, wd], F32))
                QI = ls.enter_context(nc.sbuf_tensor(f"QI{tag}", [128, wd], I32))
                QBF = [ls.enter_context(nc.sbuf_tensor(f"QB{tag}{i}", [128, wd], BF16))
                       for i in range(2)]
                RW = ls.enter_context(nc.sbuf_tensor(f"RW{tag}", [1, 4], F32))
                SCL = ls.enter_context(nc.sbuf_tensor(f"SCL{tag}", [128, 1], F32))
                P0 = ls.enter_context(nc.psum_tensor(f"P0{tag}", [128, 512], F32))

                ered = None
                for c in range(nch):
                    ei = sch.op("sync", "dma_start", WF[:, c * wd:(c + 1) * wd],
                                wsrc[c * 128:(c + 1) * 128, :], inc=16)
                    ered = sch.op("vector", "tensor_reduce", SA[:, c:c + 1],
                                  WF[:, c * wd:(c + 1) * wd], AX.X, ALU.add,
                                  apply_absolute_value=True, waits=[ei])
                sch.op("tensor", "matmul", P0[0:1, 0:nch], ONE_C_F[:, :], SA[:, :],
                       start=True, stop=True, waits=[ered])
                epe = sch.ev("tensor")
                sch.vop("tensor_reduce", RW[0:1, 0:1], P0[0:1, 0:nch],
                        AX.X, ALU.add, waits=[epe])
                sch.vop("tensor_scalar", RW[0:1, 1:2], RW[0:1, 0:1],
                        1.0 / (nch * 128 * wd), 1e-5, ALU.mult, ALU.max)
                sch.vop("reciprocal", RW[0:1, 2:3], RW[0:1, 1:2])
                edve = sch.ev("vector")
                sch.op("tensor", "matmul", P0[:, 0:1], ONE_R_F[:, :], RW[0:1, 2:3],
                       start=True, stop=True, waits=[edve])
                epe = sch.ev("tensor")
                sch.op("scalar", "activation", SCL[:, :], P0[:, 0:1], AF.Identity,
                       bias=0.0, scale=1.0, waits=[epe])
                ea = sch.ev("scalar")
                sch.op("tensor", "matmul", P0[:, 1:2], ONE_R_F[:, :], RW[0:1, 1:2],
                       start=True, stop=True, waits=[ea])
                epe = sch.ev("tensor")
                sch.op("scalar", "activation", invcol[:, :], P0[:, 1:2], AF.Identity,
                       bias=0.0, scale=1.0, waits=[epe])
                escl = sch.ev("scalar")
                edma = [None, None]
                for c in range(nch):
                    qb = QBF[c % 2]
                    sch.op("vector", "tensor_scalar", QS[:, :],
                           WF[:, c * wd:(c + 1) * wd], SCL[:, :], None, ALU.mult,
                           waits=[escl, edma[c % 2]])
                    sch.op("vector", "tensor_copy", QI[:, :], QS[:, :],
                           waits=[sch.ev("vector")])
                    eq = sch.op("vector", "tensor_scalar", qb[:, :], QI[:, :],
                                1.0, -1.0, ALU.min, ALU.max,
                                waits=[sch.ev("vector")])
                    edma[c % 2] = sch.op("sync", "dma_start", wdst[c, :, :],
                                         qb[:, :], waits=[eq], inc=16)
                ev_wq_dma = sch.ev("gpsimd")
                sch.flush(blk)

    cst = dict(IDF=IDF, IDB=IDB, SEL2=SEL2, BLK1=BLK1, ONE_C_B=ONE_C_B,
               ONE_R_B=ONE_R_B, ONE_C_F=ONE_C_F, ONE_R_F=ONE_R_F,
               BQKV=BQKV, BPROJ=BPROJ, CONVW=CONVW, CONVB=CONVB,
               GCOL=GCOL, VBIAS=VBIAS, INVW1=INVW1, INVW2=INVW2, EPSA=EPSA)
    for s in range(SEQS):
        with ExitStack() as qs_scope:
            ATM = qs_scope.enter_context(nc.sbuf_tensor(f"ATM{s}", [128, 16], F32))
            BTM = qs_scope.enter_context(nc.sbuf_tensor(f"BTM{s}", [128, 16], F32))
            DTM = qs_scope.enter_context(nc.sbuf_tensor(f"DTM{s}", [128, 16], F32))
            with ExitStack() as attn_scope:
                QH = attn_scope.enter_context(
                    nc.sbuf_tensor(f"QH{s}", [128, NCH * S], BF16))
                KH = attn_scope.enter_context(
                    nc.sbuf_tensor(f"KH{s}", [128, NCH * S], BF16))
                VT = attn_scope.enter_context(
                    nc.sbuf_tensor(f"VT{s}", [128, KC * H * 65], BF16))
                CHB = attn_scope.enter_context(
                    nc.sbuf_tensor(f"CHB{s}", [128, H], F32))
                _phase1(nc, sch, s, x_in, wqkvT, QH, KH, VT, CHB, cst)
                ev_xp_last = _phase2(nc, sch, s, x_in, wprojT, xp_s, QH, KH, VT,
                                     CHB, ATM, BTM, DTM, cst,
                                     dbg if s == 0 else None)
            _phase3(nc, sch, s, xp_s, w1q_s, w2q_s, out,
                    ATM, BTM, DTM, cst, ev_xp_last, ev_wq_dma,
                    dbg if s == 0 else None)

    with nc.Block() as blk:
        fin = sch.ev("gpsimd")
        fin2 = sch.ev("sync")
        sch.op("gpsimd", "nop", waits=[fin, fin2], inc=0)
        sch.flush(blk)

    st.close()
    return nc


def _phase1(nc, sch, s, x_in, wqkvT, QH, KH, VT, CHB, cst):
    IDF, BLK1, SEL2 = cst["IDF"], cst["BLK1"], cst["SEL2"]
    ONE_C_B, ONE_R_B, ONE_R_F = cst["ONE_C_B"], cst["ONE_R_B"], cst["ONE_R_F"]
    BQKV, VBIAS, GCOL = cst["BQKV"], cst["VBIAS"], cst["GCOL"]
    EPSA = cst["EPSA"]
    with nc.Block() as blk:
        with ExitStack() as ls:
            WQKV = ls.enter_context(nc.sbuf_tensor(f"WQKV{s}", [128, NCH * 3 * C], BF16))
            WSTG = ls.enter_context(nc.sbuf_tensor(f"WSTG{s}", [128, 3 * C // 2], F32))
            XSTG = [ls.enter_context(nc.sbuf_tensor(f"XSTG{s}0", [128, TT], F32))] * 2
            XBF = ls.enter_context(nc.sbuf_tensor(f"XBF{s}", [128, NCH * TT], BF16))
            SQ = [ls.enter_context(nc.sbuf_tensor(f"SQP{s}{i}", [128, TT], BF16))
                  for i in range(2)]
            LN = [ls.enter_context(nc.sbuf_tensor(f"LN{s}{i}", [128, NCH * TT], BF16))
                  for i in range(2)]
            TMPF = ls.enter_context(nc.sbuf_tensor(f"TMPF{s}", [128, TT], F32))
            ABC = ls.enter_context(nc.sbuf_tensor(f"ABC{s}", [128, 2 * TT], F32))
            ROWS = ls.enter_context(nc.sbuf_tensor(f"ROWS{s}", [1, 4 * TT], F32))
            RS2 = ls.enter_context(nc.sbuf_tensor(f"RS2{s}", [2, TT], F32))
            RL2F = ls.enter_context(nc.sbuf_tensor(f"RL2F{s}", [2, TT], F32))
            RH2 = ls.enter_context(nc.sbuf_tensor(f"RH2{s}", [2, TT], BF16))
            RL2 = ls.enter_context(nc.sbuf_tensor(f"RL2{s}", [2, TT], BF16))
            RHB = ls.enter_context(nc.sbuf_tensor(f"RHB{s}", [1, TT], BF16))
            RLB = ls.enter_context(nc.sbuf_tensor(f"RLB{s}", [1, TT], BF16))
            CT = ls.enter_context(nc.sbuf_tensor(f"CT{s}", [1, 2], F32))
            CSQ = ls.enter_context(nc.sbuf_tensor(f"CSQ{s}", [2, 1], F32))
            CSQ2 = ls.enter_context(nc.sbuf_tensor(f"CSQ2{s}", [2, 1], F32))
            KNACC = ls.enter_context(nc.sbuf_tensor(f"KNACC{s}", [2, NCH], F32))
            PQK = [ls.enter_context(nc.psum_tensor(f"PQK{s}{i}", [128, TT], F32))
                   for i in range(2)]
            PV = [ls.enter_context(nc.psum_tensor(f"PVx{s}{i}", [128, TT], F32))
                  for i in range(2)]
            PST1 = ls.enter_context(nc.psum_tensor(f"PST1{s}", [1, TT], F32))
            PST2 = ls.enter_context(nc.psum_tensor(f"PST2{s}", [1, TT], F32))
            PRMS = ls.enter_context(nc.psum_tensor(f"PRMS{s}", [2, TT], F32))
            PBC = ls.enter_context(nc.psum_tensor(f"PBCp1{s}", [128, TT], F32))

            MR = ROWS[0:1, 0:TT]
            MSQ = ROWS[0:1, TT:2 * TT]
            RSTD = ROWS[0:1, 2 * TT:3 * TT]
            SCR = ROWS[0:1, 3 * TT:4 * TT]
            VAR = MSQ
            BRN = SCR
            SDR = SCR
            TR1 = SCR

            sch.op("scalar", "load_act_set", 6)

            ecast = None
            HW_ = 3 * C // 2
            for dc in range(NCH):
                for hf in range(2):
                    ed = sch.op("sync", "dma_start", WSTG[:, :],
                                wqkvT[dc * 128:(dc + 1) * 128,
                                      hf * HW_:(hf + 1) * HW_],
                                waits=[ecast], inc=16)
                    ecast = sch.op("gpsimd", "tensor_copy",
                                   WQKV[:, dc * 3 * C + hf * HW_:
                                        dc * 3 * C + (hf + 1) * HW_],
                                   WSTG[:, :], waits=[ed])
            ew_ready = ecast
            eones = sch.op("gpsimd", "memset",
                           VT[:, :].rearrange("p (k h q) -> p k h q", k=KC, h=H)[:, :, :, 64:65],
                           1.0)
            ekn0 = sch.op("vector", "memset", KNACC[:, :], 0.0)

            ev_qkv_evac = [None, None]
            ev_v_evac = [None, None]
            ev_ln_pe_done = [None, None]
            ev_stats_read = None
            ev_xbf_free = []
            ev_rms_pe = None
            exstg = [None, None]

            for tt in range(NTT):
                par = tt % 2
                exc = None
                for dc in range(NCH):
                    ed = sch.op("sync", "dma_start", XSTG[dc % 2][:, :],
                                x_in[s, dc * 128:(dc + 1) * 128, tt * TT:(tt + 1) * TT],
                                waits=ev_xbf_free + [exstg[dc % 2]], inc=16)
                    exc = sch.op("gpsimd", "tensor_copy",
                                 XBF[:, dc * TT:(dc + 1) * TT], XSTG[dc % 2][:, :],
                                 waits=[ed])
                    exstg[0] = exstg[1] = exc
                ev_xbf_done = exc
                esq = [None, None]
                epe_sq = [None, None]
                for dc in range(NCH):
                    esq[dc % 2] = sch.op("scalar", "activation", SQ[dc % 2][:, :],
                                         XBF[:, dc * TT:(dc + 1) * TT], AF.Square,
                                         bias=0.0, scale=1.0,
                                         waits=[ev_xbf_done, epe_sq[dc % 2],
                                                ev_rms_pe])
                    sch.op("tensor", "matmul", PST1[:, :], ONE_C_B[:, :],
                           XBF[:, dc * TT:(dc + 1) * TT],
                           start=(dc == 0), stop=(dc == NCH - 1),
                           waits=[ev_xbf_done, ev_stats_read])
                    sch.op("tensor", "matmul", PST2[:, :], ONE_C_B[:, :],
                           SQ[dc % 2][:, :], start=(dc == 0), stop=(dc == NCH - 1),
                           waits=[esq[dc % 2]])
                    epe_sq[dc % 2] = sch.ev("tensor")
                epe_stats = sch.ev("tensor")
                esq_last = sch.ev("scalar")
                sch.op("scalar", "activation", MR, PST1[0:1, :], AF.Identity,
                       bias=0.0, scale=1.0 / C, waits=[epe_stats])
                sch.op("scalar", "activation", MSQ, PST2[0:1, :], AF.Identity,
                       bias=0.0, scale=1.0 / C)
                ea = sch.ev("scalar")
                ev_stats_read = ea
                sch.op("vector", "tensor_mul", TR1, MR, MR, waits=[ea])
                sch.op("vector", "tensor_sub", VAR, MSQ, TR1)
                edve = sch.ev("vector")
                sch.op("scalar", "activation", SDR, VAR, AF.Ln,
                       bias=EPSA[0:1, 0:1], scale=1.0, waits=[edve])
                sch.op("scalar", "activation", RSTD, SDR, AF.Exp,
                       bias=0.0, scale=-0.5, waits=[sch.ev("scalar")])
                ea = sch.ev("scalar")
                sch.op("vector", "tensor_copy", RHB[:, :], RSTD, waits=[ea])
                sch.op("vector", "tensor_sub", RLB[:, :], RSTD, RHB[:, :])
                ehl = sch.ev("vector")
                eA = _bc_rows(sch, PBC, ABC[:, 0:TT], RHB, RLB, ONE_R_B, [ehl])
                sch.op("vector", "tensor_mul", BRN, MR, RSTD)
                sch.op("vector", "tensor_scalar", BRN, BRN, -1.0, None, ALU.mult)
                sch.op("vector", "tensor_copy", RHB[:, :], BRN, waits=[eA])
                sch.op("vector", "tensor_sub", RLB[:, :], BRN, RHB[:, :])
                ehl = sch.ev("vector")
                eB = _bc_rows(sch, PBC, ABC[:, TT:2 * TT], RHB, RLB, ONE_R_B,
                              [ehl, eA])
                eln = None
                for dc in range(NCH):
                    sch.op("vector", "tensor_mul", TMPF[:, :],
                           XBF[:, dc * TT:(dc + 1) * TT], ABC[:, 0:TT],
                           waits=[eB, ev_ln_pe_done[par]])
                    eln = sch.op("vector", "tensor_add",
                                 LN[par][:, dc * TT:(dc + 1) * TT], TMPF[:, :],
                                 ABC[:, TT:2 * TT])
                ev_xbf_free = [eln, epe_stats, esq_last]
                for oc in range(16):
                    pq = PQK[oc % 2]
                    for dc in range(NCH):
                        w = [eln, ew_ready, ev_qkv_evac[oc % 2]] if dc == 0 else ()
                        sch.op("tensor", "matmul", pq[:, :],
                               WQKV[:, dc * 3 * C + oc * 128:dc * 3 * C + (oc + 1) * 128],
                               LN[par][:, dc * TT:(dc + 1) * TT],
                               start=(dc == 0), stop=(dc == NCH - 1), waits=w)
                    epe = sch.ev("tensor")
                    dst = QH if oc < 8 else KH
                    cc = oc % 8
                    ev_qkv_evac[oc % 2] = sch.op(
                        "scalar", "activation",
                        dst[:, cc * S + tt * TT:cc * S + (tt + 1) * TT], pq[:, :],
                        AF.Identity, bias=BQKV[:, oc:oc + 1], scale=1.0, waits=[epe])
                for tsl in range(TT // 128):
                    tc = tt * (TT // 128) + tsl
                    for half in range(2):
                        idx = (tsl * 2 + half) % 2
                        for dc in range(NCH):
                            w = [eln, ew_ready, eones, ev_v_evac[idx]] if dc == 0 else ()
                            sch.op("tensor", "matmul", PV[idx][:, :],
                                   LN[par][:, dc * TT + tsl * 128:dc * TT + (tsl + 1) * 128],
                                   WQKV[:, dc * 3 * C + 2 * C + half * 512:
                                        dc * 3 * C + 2 * C + (half + 1) * 512],
                                   start=(dc == 0), stop=(dc == NCH - 1), waits=w)
                        epe = sch.ev("tensor")
                        vt_dst = VT[:, tc * H * 65 + half * 8 * 65:
                                    tc * H * 65 + (half + 1) * 8 * 65]
                        vt_dst = vt_dst.rearrange("p (h q) -> p h q", q=65)[:, :, 0:64]
                        vsrc = PV[idx][:, :].rearrange("p (h q) -> p h q", q=64)
                        vb = VBIAS[:, half * 512:(half + 1) * 512].rearrange(
                            "p (h q) -> p h q", q=64)
                        ev_v_evac[idx] = sch.op("vector", "tensor_tensor",
                                                vt_dst, vsrc, vb, ALU.add, waits=[epe])
                ev_ln_pe_done[par] = sch.ev("tensor")
                ev_bc_read = None
                ev_prms = None
                for ch in range(NCH):
                    for which in ("q", "k"):
                        buf = QH if which == "q" else KH
                        sl = buf[:, ch * S + tt * TT:ch * S + (tt + 1) * TT]
                        esq2 = sch.op("scalar", "activation", SQ[0][:, :], sl,
                                      AF.Square, bias=0.0, scale=1.0,
                                      waits=[ev_qkv_evac[0], ev_qkv_evac[1],
                                             ev_rms_pe, ev_prms])
                        sch.op("tensor", "matmul", PRMS[:, :], BLK1[:, :],
                               SQ[0][:, :], start=True, stop=True,
                               waits=[esq2, ev_bc_read])
                        epe = sch.ev("tensor")
                        ev_prms = epe
                        sch.op("scalar", "activation", RL2F[:, :], PRMS[0:2, :],
                               AF.Ln, bias=EPSA[0:2, 1:2], scale=1.0 / D,
                               waits=[epe])
                        sch.op("scalar", "activation", RS2[:, :], RL2F[:, :],
                               AF.Exp, bias=0.0, scale=-0.5,
                               waits=[sch.ev("scalar")])
                        ea = sch.ev("scalar")
                        sch.op("vector", "tensor_copy", RH2[:, :], RS2[:, :], waits=[ea])
                        sch.op("vector", "tensor_sub", RL2[:, :], RS2[:, :], RH2[:, :])
                        edve = sch.ev("vector")
                        sch.op("tensor", "matmul", PBC[:, :], SEL2[:, :], RH2[:, :],
                               start=True, stop=False, waits=[edve])
                        sch.op("tensor", "matmul", PBC[:, :], SEL2[:, :], RL2[:, :],
                               start=False, stop=True)
                        epe = sch.ev("tensor")
                        if which == "q":
                            ev_bc_read = sch.op("vector", "tensor_tensor", sl, sl,
                                                PBC[:, :], ALU.mult, waits=[epe])
                        else:
                            ev_bc_read = sch.op("vector", "scalar_tensor_tensor",
                                                sl, sl, GCOL[:, :], PBC[:, :],
                                                ALU.mult, ALU.mult, waits=[epe])
                            esq3 = sch.op("scalar", "activation", SQ[1][:, :], sl,
                                          AF.Square, bias=0.0, scale=1.0,
                                          waits=[ev_bc_read, ev_prms])
                            sch.op("tensor", "matmul", PRMS[:, :], BLK1[:, :],
                                   SQ[1][:, :], start=True, stop=True, waits=[esq3])
                            epe2 = sch.ev("tensor")
                            ev_prms = epe2
                            sch.vop("tensor_reduce", CSQ[:, :],
                                    PRMS[0:2, :], AX.X, ALU.max, waits=[epe2, ekn0])
                            sch.vop("tensor_tensor", KNACC[:, ch:ch + 1],
                                    KNACC[:, ch:ch + 1], CSQ[:, :], ALU.max)
                        ev_rms_pe = sch.ev("tensor")
            for ch in range(NCH):
                edve = sch.vop("tensor_copy", CSQ[:, :], KNACC[:, ch:ch + 1])
                sch.op("scalar", "activation", CSQ2[:, :], CSQ[:, :], AF.Ln,
                       bias=0.0, scale=1.0, waits=[edve])
                sch.op("scalar", "activation", CSQ[:, :], CSQ2[:, :], AF.Exp,
                       bias=0.0, scale=0.5, waits=[sch.ev("scalar")])
                ea = sch.ev("scalar")
                sch.op("tensor", "transpose", PRMS[0:1, 0:2], CSQ[:, :],
                       IDF[0:2, 0:2], waits=[ea])
                epe = sch.ev("tensor")
                sch.op("scalar", "activation", CT[:, :], PRMS[0:1, 0:2], AF.Identity,
                       bias=0.0, scale=QBOUND, waits=[epe])
                ea2 = sch.ev("scalar")
                for hl in range(2):
                    sch.op("tensor", "matmul", PBC[:, 0:1], ONE_R_F[:, :],
                           CT[0:1, hl:hl + 1], start=True, stop=True, waits=[ea2])
                    epe = sch.ev("tensor")
                    eneg = sch.op("vector", "tensor_scalar",
                                  CHB[:, 2 * ch + hl:2 * ch + hl + 1], PBC[:, 0:1],
                                  -1.0, None, ALU.mult, waits=[epe])
                    sch.op("tensor", "nop", waits=[eneg], inc=0)
            sch.flush(blk)


def _bc_rows(sch, PBC, dst_ap, RHB, RLB, ONE_R_B, waits):
    sch.op("tensor", "matmul", PBC[:, :], ONE_R_B[:, :], RHB[:, :],
           start=True, stop=False, waits=list(waits))
    sch.op("tensor", "matmul", PBC[:, :], ONE_R_B[:, :], RLB[:, :],
           start=False, stop=True)
    epe = sch.ev("tensor")
    return sch.op("scalar", "activation", dst_ap, PBC[:, :], AF.Identity,
                  bias=0.0, scale=1.0, waits=[epe])


def _phase2(nc, sch, s, x_in, wprojT, xp_s, QH, KH, VT, CHB,
            ATM, BTM, DTM, cst, dbg=None):
    IDF = cst["IDF"]
    ONE_C_B, ONE_R_B, ONE_R_F = cst["ONE_C_B"], cst["ONE_R_B"], cst["ONE_R_F"]
    BPROJ, INVW1, EPSA = cst["BPROJ"], cst["INVW1"], cst["EPSA"]
    with nc.Block() as blk:
        with ExitStack() as ls:
            PTL = [ls.enter_context(nc.sbuf_tensor(f"PTL{s}{i}", [128, 2 * TT], BF16))
                   for i in range(2)]
            WPJ = ls.enter_context(nc.sbuf_tensor(f"WPJ{s}", [128, NCH * C], BF16))
            WPS = ls.enter_context(nc.sbuf_tensor(f"WPS{s}", [128, C], F32))
            CTX = ls.enter_context(nc.sbuf_tensor(f"CTX{s}", [128, NCH * TT], BF16))
            XRC = [ls.enter_context(nc.sbuf_tensor(f"XRC{s}{i}", [128, TT], F32))
                   for i in range(2)]
            XPC = [ls.enter_context(nc.sbuf_tensor(f"XPC{s}{i}", [128, TT], F32))
                   for i in range(2)]
            MAXA = ls.enter_context(nc.sbuf_tensor(f"MAXA{s}", [128, TT], F32))
            MINA = ls.enter_context(nc.sbuf_tensor(f"MINA{s}", [128, TT], F32))
            XBF2 = ls.enter_context(nc.sbuf_tensor(f"XBF2{s}", [128, TT], BF16))
            SQ3 = ls.enter_context(nc.sbuf_tensor(f"SQ3{s}", [128, TT], BF16))
            DBC = ls.enter_context(nc.sbuf_tensor(f"DBC{s}", [64, TT], F32))
            DNR = ls.enter_context(nc.sbuf_tensor(f"DNR{s}", [1, 2 * TT], F32))
            DNH = ls.enter_context(nc.sbuf_tensor(f"DNH{s}", [1, TT], BF16))
            DNL = ls.enter_context(nc.sbuf_tensor(f"DNL{s}", [1, TT], BF16))
            MTM = ls.enter_context(nc.sbuf_tensor(f"MTM{s}", [128, 16], F32))
            RSTM = ls.enter_context(nc.sbuf_tensor(f"RSTM{s}", [128, 16], F32))
            MXTM = ls.enter_context(nc.sbuf_tensor(f"MXTM{s}", [128, 16], F32))
            MNTM = ls.enter_context(nc.sbuf_tensor(f"MNTM{s}", [128, 16], F32))
            TM1 = ls.enter_context(nc.sbuf_tensor(f"TM1{s}", [128, 16], F32))
            TM2 = ls.enter_context(nc.sbuf_tensor(f"TM2{s}", [128, 16], F32))
            ROWS2 = ls.enter_context(nc.sbuf_tensor(f"ROWS2{s}", [1, 4 * TT], F32))
            PK = [ls.enter_context(nc.psum_tensor(f"PK{s}{i}", [128, TT], F32))
                  for i in range(8)]

            MR2 = ROWS2[0:1, 0:TT]
            MSQ2 = ROWS2[0:1, TT:2 * TT]
            RST2 = ROWS2[0:1, 2 * TT:3 * TT]
            TR2 = ROWS2[0:1, 3 * TT:4 * TT]
            VAR2 = MSQ2

            sch.op("scalar", "load_act_set", 6)

            ecast = None
            for ci in range(NCH):
                ed = sch.op("sync", "dma_start", WPS[:, :],
                            wprojT[ci * 128:(ci + 1) * 128, :],
                            waits=[ecast], inc=16)
                ecast = sch.op("gpsimd", "tensor_copy",
                               WPJ[:, ci * C:(ci + 1) * C], WPS[:, :], waits=[ed])
            ewp = ecast
            einit = None

            ev_ctx_done = None
            ev_proj_evac = None
            ev_stats_read = None
            ev_stats_mm = None
            ev_xpc = [None, None]
            ev_xrc = [None, None]
            emn = None

            for qt in range(NTT):
                PSA = [PK[0], PK[1]]
                PSB = [PK[2], PK[3]]
                PSO = [PK[4], PK[5]]
                PSBC = PK[6]
                ev_exp = {}
                ev_pv = {}
                ev_exp_slot = [None, None]
                ev_pv_slot = [None, None]
                for pair in range(8):
                    qslA = QH[0:64, pair * S + qt * TT:pair * S + (qt + 1) * TT]
                    qslB = QH[64:128, pair * S + qt * TT:pair * S + (qt + 1) * TT]
                    for kc in range(KC + 1):
                        if kc < KC:
                            kslA = KH[0:64, pair * S + kc * 128:pair * S + (kc + 1) * 128]
                            kslB = KH[64:128, pair * S + kc * 128:pair * S + (kc + 1) * 128]
                            w = [ev_exp_slot[kc % 2]]
                            if kc == 0:
                                w += [ev_proj_evac]
                            sch.op("tensor", "matmul", PSA[kc % 2][:, :], kslA, qslA,
                                   start=True, stop=True, tile_position=(0, 0), waits=w)
                            sch.op("tensor", "matmul", PSB[kc % 2][:, :], kslB, qslB,
                                   start=True, stop=True, tile_position=(64, 0))
                            epe = sch.ev("tensor")
                        if kc > 0:
                            pk = kc - 1
                            for hl in range(2):
                                g = 2 * pair + hl
                                w = [ev_exp[pk]]
                                if pk == 0:
                                    w.append(ev_ctx_done)
                                sch.op("tensor", "matmul", PSO[hl][0:65, :],
                                       VT[:, pk * H * 65 + g * 65:pk * H * 65 + (g + 1) * 65],
                                       PTL[hl][:, (pk % 2) * TT:(pk % 2 + 1) * TT],
                                       start=(pk == 0), stop=(pk == KC - 1), waits=w)
                            ev_pv[pk] = sch.ev("tensor")
                            ev_pv_slot[pk % 2] = ev_pv[pk]
                        if kc < KC:
                            w = [epe, ev_pv_slot[kc % 2]]
                            sch.op("scalar", "activation",
                                   PTL[0][:, (kc % 2) * TT:(kc % 2 + 1) * TT],
                                   PSA[kc % 2][:, :],
                                   AF.Exp, bias=CHB[:, 2 * pair:2 * pair + 1],
                                   scale=0.125, waits=w)
                            ev_exp[kc] = sch.op("scalar", "activation",
                                                PTL[1][:, (kc % 2) * TT:(kc % 2 + 1) * TT],
                                                PSB[kc % 2][:, :], AF.Exp,
                                                bias=CHB[:, 2 * pair + 1:2 * pair + 2],
                                                scale=0.125)
                            ev_exp_slot[kc % 2] = ev_exp[kc]
                    epe_pv = sch.ev("tensor")
                    for hl in range(2):
                        g = 2 * pair + hl
                        sch.op("vector", "tensor_scalar", DNR[0:1, 0:TT],
                               PSO[hl][64:65, :], 1e-30, None, ALU.max,
                               waits=[epe_pv])
                        edn = sch.ev("vector")
                        sch.op("scalar", "activation", DNR[0:1, TT:2 * TT],
                               DNR[0:1, 0:TT], AF.Ln, bias=0.0, scale=1.0,
                               waits=[edn])
                        sch.op("scalar", "activation", DNR[0:1, TT:2 * TT],
                               DNR[0:1, TT:2 * TT], AF.Exp, bias=0.0, scale=-1.0,
                               waits=[sch.ev("scalar")])
                        eact_dn = sch.ev("scalar")
                        sch.op("vector", "tensor_copy", DNH[:, :], DNR[0:1, TT:2 * TT],
                               waits=[eact_dn])
                        sch.op("vector", "tensor_sub", DNL[:, :], DNR[0:1, TT:2 * TT],
                               DNH[:, :])
                        edve = sch.ev("vector")
                        sch.op("tensor", "matmul", PSBC[0:64, :],
                               ONE_R_B[0:1, 0:64], DNH[:, :],
                               start=True, stop=False, waits=[edve])
                        sch.op("tensor", "matmul", PSBC[0:64, :],
                               ONE_R_B[0:1, 0:64], DNL[:, :],
                               start=False, stop=True)
                        epe_bc = sch.ev("tensor")
                        ebc = sch.op("vector", "tensor_copy", DBC[:, :],
                                     PSBC[0:64, :], waits=[epe_bc, ev_ctx_done])
                        ctx_dst = CTX[64 * hl:64 * hl + 64,
                                      pair * TT:(pair + 1) * TT]
                        ev_ctx_done = sch.op("vector", "tensor_tensor", ctx_dst,
                                             PSO[hl][0:64, :], DBC[:, :], ALU.mult,
                                             waits=[ebc])
                PSP = [PK[0], PK[1]]
                PSS1 = PK[2][0:1, :]
                PSS2 = PK[3][0:1, :]
                PTRP = PK[6]
                ev_sq3 = None
                for co in range(NCH):
                    edx = sch.op("sync", "dma_start", XRC[co % 2][:, :],
                                 x_in[s, co * 128:(co + 1) * 128, qt * TT:(qt + 1) * TT],
                                 waits=[ev_xrc[co % 2]], inc=16)
                    for ci in range(NCH):
                        w = [ev_ctx_done, ewp, ev_proj_evac] if ci == 0 else ()
                        sch.op("tensor", "matmul", PSP[co % 2][:, :],
                               WPJ[:, ci * C + co * 128:ci * C + (co + 1) * 128],
                               CTX[:, ci * TT:(ci + 1) * TT],
                               start=(ci == 0), stop=(ci == NCH - 1), waits=w)
                    epe = sch.ev("tensor")
                    ev_proj_evac = sch.op("vector", "scalar_tensor_tensor",
                                          XPC[co % 2][:, :], PSP[co % 2][:, :],
                                          BPROJ[:, co:co + 1], XRC[co % 2][:, :],
                                          ALU.add, ALU.add,
                                          waits=[epe, edx] + (ev_xpc[co % 2] or []))
                    exdma = sch.op("sync", "dma_start",
                                   xp_s[s, co, :, qt * TT:(qt + 1) * TT],
                                   XPC[co % 2][:, :], waits=[ev_proj_evac], inc=16)
                    if co == 0:
                        emx = sch.op("vector", "tensor_copy", MAXA[:, :],
                                     XPC[co % 2][:, :],
                                     waits=[ev_proj_evac, einit])
                        emn = sch.op("vector", "tensor_copy", MINA[:, :],
                                     XPC[co % 2][:, :])
                    else:
                        emx = sch.op("vector", "tensor_tensor", MAXA[:, :],
                                     MAXA[:, :], XPC[co % 2][:, :], ALU.max,
                                     waits=[ev_proj_evac])
                        emn = sch.op("vector", "tensor_tensor", MINA[:, :],
                                     MINA[:, :], XPC[co % 2][:, :], ALU.min)
                    ecast2 = sch.op("vector", "tensor_copy", XBF2[:, :],
                                    XPC[co % 2][:, :], waits=[ev_stats_mm])
                    ev_xpc[co % 2] = [emn, exdma]
                    ev_xrc[co % 2] = ev_proj_evac
                    ev_sq3 = sch.op("scalar", "activation", SQ3[:, :], XBF2[:, :],
                                    AF.Square, bias=0.0, scale=1.0,
                                    waits=[ecast2, ev_stats_mm])
                    sch.op("tensor", "matmul", PSS1, ONE_C_B[:, :], XBF2[:, :],
                           start=(co == 0), stop=(co == NCH - 1),
                           waits=[ecast2, ev_stats_read])
                    sch.op("tensor", "matmul", PSS2, ONE_C_B[:, :], SQ3[:, :],
                           start=(co == 0), stop=(co == NCH - 1), waits=[ev_sq3])
                    ev_stats_mm = sch.ev("tensor")
                epe_st = sch.ev("tensor")
                sch.op("scalar", "activation", MR2, PSS1, AF.Identity, bias=0.0,
                       scale=1.0 / C, waits=[epe_st])
                sch.op("scalar", "activation", MSQ2, PSS2, AF.Identity, bias=0.0,
                       scale=1.0 / C)
                ea = sch.ev("scalar")
                ev_stats_read = ea
                sch.op("vector", "tensor_mul", TR2, MR2, MR2, waits=[ea])
                sch.op("vector", "tensor_sub", VAR2, MSQ2, TR2)
                edve = sch.ev("vector")
                sch.op("scalar", "activation", TR2, VAR2, AF.Ln,
                       bias=EPSA[0:1, 0:1], scale=1.0, waits=[edve])
                sch.op("scalar", "activation", RST2, TR2, AF.Exp, bias=0.0,
                       scale=-0.5)
                ea2 = sch.ev("scalar")
                for j in range(4):
                    col = qt * 4 + j
                    for (rowap, tmdst) in ((MR2, MTM), (RST2, RSTM)):
                        sch.op("tensor", "transpose", PTRP[0:128, 0:1],
                               rowap[0:1, j * 128:(j + 1) * 128], IDF[0:1, 0:1],
                               waits=[ea2])
                        epe = sch.ev("tensor")
                        eev = sch.op("scalar", "activation",
                                     tmdst[:, col:col + 1], PTRP[0:128, 0:1],
                                     AF.Identity, bias=0.0, scale=1.0, waits=[epe])
                        sch.op("tensor", "nop", waits=[eev], inc=0)
                    for (acc, tmdst, op_) in ((MAXA, MXTM, ALU.max),
                                              (MINA, MNTM, ALU.min)):
                        sch.op("tensor", "transpose", PTRP[:, 0:128],
                               acc[:, j * 128:(j + 1) * 128], IDF[:, :],
                               waits=[emx, emn])
                        epe = sch.ev("tensor")
                        erd = sch.op("vector", "tensor_reduce",
                                     tmdst[:, col:col + 1], PTRP[:, 0:128],
                                     AX.X, op_, waits=[epe])
                        sch.op("tensor", "nop", waits=[erd], inc=0)
                erd_last = sch.ev("vector")
                einit = erd_last

            w0 = [sch.ev("vector"), sch.ev("scalar")]
            sch.vop("tensor_sub", TM1[:, :], MXTM[:, :], MTM[:, :], waits=w0)
            sch.vop("tensor_sub", TM2[:, :], MTM[:, :], MNTM[:, :])
            sch.vop("tensor_tensor", TM1[:, :], TM1[:, :], TM2[:, :], ALU.max)
            sch.vop("tensor_tensor", TM1[:, :], TM1[:, :], RSTM[:, :], ALU.mult)
            sch.vop("tensor_scalar", TM1[:, :], TM1[:, :], 1e-5, None, ALU.max)
            sch.vop("reciprocal", TM2[:, :], TM1[:, :])
            sch.vop("tensor_tensor", TM2[:, :], TM2[:, :], RSTM[:, :], ALU.mult)
            sch.vop("tensor_scalar", ATM[:, :], TM2[:, :], 127.0, None, ALU.mult)
            sch.vop("tensor_tensor", BTM[:, :], MTM[:, :], ATM[:, :], ALU.mult)
            sch.vop("tensor_scalar", BTM[:, :], BTM[:, :], -1.0, None, ALU.mult)
            sch.vop("tensor_scalar", DTM[:, :], TM1[:, :], INVW1[:, :],
                    1.0 / 127.0, ALU.mult, ALU.mult)
            if dbg is not None:
                edv = sch.ev("vector")
                for i_, tdbg in enumerate((MTM, RSTM, MXTM, MNTM, ATM, BTM, DTM, TM1)):
                    sch.op("sync", "dma_start", dbg["tmr"][i_, :, :],
                           tdbg[:, :], waits=[edv], inc=16)
            ev_xp_last = sch.ev("gpsimd")
            sch.flush(blk)
    return ev_xp_last


def _phase3(nc, sch, s, xp_s, w1q_s, w2q_s, out, ATM, BTM, DTM, cst,
            ev_xp_last, ev_wq_dma, dbg=None):
    IDF, ONE_R_B, ONE_R_F = cst["IDF"], cst["ONE_R_B"], cst["ONE_R_F"]
    CONVW, CONVB, INVW2 = cst["CONVW"], cst["CONVB"], cst["INVW2"]
    with nc.Block() as blk:
        with ExitStack() as ls:
            W1Q = ls.enter_context(nc.sbuf_tensor(f"W1Q{s}", [128, NCH * DH], BF16))
            XTC = [ls.enter_context(nc.sbuf_tensor(f"XTC{s}{i}", [128, TT], F32))
                   for i in range(2)]
            XQ2 = ls.enter_context(nc.sbuf_tensor(f"XQ2{s}", [128, NCH * TT], BF16))
            TMQ = ls.enter_context(nc.sbuf_tensor(f"TMQ{s}", [128, TT], F32))
            QI3 = ls.enter_context(nc.sbuf_tensor(f"QI3{s}", [128, TT], I32))
            STT_ = ls.enter_context(nc.sbuf_tensor(f"STT{s}", [128, NHC * TT], F32))
            HCH = [ls.enter_context(nc.sbuf_tensor(f"HCH{s}{i}", [128, TT + 2], F32))
                   for i in range(2)]
            HQ = [ls.enter_context(nc.sbuf_tensor(f"HQ{s}{i}", [128, TT], BF16))
                  for i in range(2)]
            W2S = [ls.enter_context(nc.sbuf_tensor(f"W2S{s}{i}", [128, C], BF16))
                   for i in range(2)]
            EDGE = ls.enter_context(nc.sbuf_tensor(f"EDGE{s}", [128, 2 * NHC], F32))
            AM2 = ls.enter_context(nc.sbuf_tensor(f"AM2{s}", [128, TT], F32))
            MN3 = ls.enter_context(nc.sbuf_tensor(f"MN3{s}", [128, TT], F32))
            MN3T = ls.enter_context(nc.sbuf_tensor(f"MN3T{s}", [128, 4], F32))
            M2TM = ls.enter_context(nc.sbuf_tensor(f"M2TM{s}", [128, 4], F32))
            D2TM = ls.enter_context(nc.sbuf_tensor(f"D2TM{s}", [128, 4], F32))
            A2R = ls.enter_context(nc.sbuf_tensor(f"A2R{s}", [1, TT], F32))
            D2R = ls.enter_context(nc.sbuf_tensor(f"D2R{s}", [1, TT], F32))
            R3 = ls.enter_context(nc.sbuf_tensor(f"R3{s}", [1, TT], F32))
            RHB3 = ls.enter_context(nc.sbuf_tensor(f"RHB3{s}", [1, TT], BF16))
            RLB3 = ls.enter_context(nc.sbuf_tensor(f"RLB3{s}", [1, TT], BF16))
            ABC3 = ls.enter_context(nc.sbuf_tensor(f"ABC3{s}", [128, 3 * TT], F32))
            A2BC = ls.enter_context(nc.sbuf_tensor(f"A2BC{s}", [128, TT], F32))
            D2BC = ls.enter_context(nc.sbuf_tensor(f"D2BC{s}", [128, TT], F32))
            XOC = [ls.enter_context(nc.sbuf_tensor(f"XOC{s}{i}", [128, TT], F32))
                   for i in range(2)]
            XR3 = [ls.enter_context(nc.sbuf_tensor(f"XR3{s}{i}", [128, TT], F32))
                   for i in range(2)]
            SS = ls.enter_context(nc.sbuf_tensor(f"SSx{s}", [128, NHC], F32))
            HQS = ls.enter_context(nc.sbuf_tensor(f"HQS{s}", [128, NHC], BF16))
            XRS = ls.enter_context(nc.sbuf_tensor(f"XRS{s}", [128, 8], F32))
            XOS = ls.enter_context(nc.sbuf_tensor(f"XOS{s}", [128, 8], F32))
            PK = [ls.enter_context(nc.psum_tensor(f"PK3{s}{i}", [128, TT], F32))
                  for i in range(8)]

            sch.op("scalar", "preload_act_table", AF.Silu)

            ew1 = None
            for dc in range(NCH):
                ew1 = sch.op("sync", "dma_start", W1Q[:, dc * DH:(dc + 1) * DH],
                             w1q_s[dc, :, :], waits=[ev_wq_dma], inc=16)
            eedge = sch.op("vector", "memset", EDGE[:, :], 0.0)

            ev_xq_used_h = [None]
            ev_hch_used = [None, None]
            ev_hq_used = [None, None]
            ev_w2s = [None, None]
            ev_xtc = [None, None]
            ev_xoc = [None, None]
            ev_xr3 = [None, None]
            ev_st_used = None
            ev_edge_used = None
            ev_w2_evac = None
            eam_last = None

            PSW1 = [PK[0], PK[1]]
            PTR3 = PK[2]
            PBC3 = PK[3]
            state = {}

            def emit_bcq(ti, wait_pbc3):
                t0 = ti * TT
                ebc_done = None
                for k, tm in enumerate((ATM, BTM, DTM)):
                    for j in range(4):
                        col = ti * 4 + j
                        sch.op("tensor", "matmul", PBC3[:, j * 128:(j + 1) * 128],
                               tm[:, col:col + 1].broadcast_to((128, 128)),
                               IDF[:, :], start=True, stop=True,
                               waits=wait_pbc3 + [ebc_done] if j == 0 else ())
                    epe = sch.ev("tensor")
                    ebc_done = sch.op("scalar", "activation",
                                      ABC3[:, k * TT:(k + 1) * TT], PBC3[:, :],
                                      AF.Identity, bias=0.0, scale=1.0, waits=[epe])
                eabc = ebc_done
                for dc in range(NCH):
                    ed = sch.op("sync", "dma_start", XTC[dc % 2][:, :],
                                xp_s[s, dc, :, t0:t0 + TT],
                                waits=[ev_xtc[dc % 2], ev_xp_last], inc=16)
                    sch.op("vector", "tensor_mul", TMQ[:, :], XTC[dc % 2][:, :],
                           ABC3[:, 0:TT], waits=[ed, eabc, ev_xq_used_h[0]])
                    sch.op("vector", "tensor_add", TMQ[:, :], TMQ[:, :],
                           ABC3[:, TT:2 * TT], waits=[sch.ev("vector")])
                    sch.op("vector", "tensor_copy", QI3[:, :], TMQ[:, :],
                           waits=[sch.ev("vector")])
                    ev_xtc[dc % 2] = sch.op("vector", "tensor_scalar",
                                            XQ2[:, dc * TT:(dc + 1) * TT], QI3[:, :],
                                            127.0, -128.0, ALU.min, ALU.max,
                                            waits=[sch.ev("vector")])
                exq = sch.ev("vector")
                if dbg is not None:
                    exq = sch.op("sync", "dma_start", dbg["xq2a"][ti, :, :],
                                 XQ2[:, :], waits=[exq], inc=16)
                return (eabc, exq)

            for ti in range(NTT):
                t0 = ti * TT
                if ti not in state:
                    state[ti] = emit_bcq(ti, [ev_w2_evac])
                eabc_t, exq = state[ti]
                eam_prev_tile = eam_last
                econv_prev = [None, None]
                esilu = None
                sch.serial = "w1conv" in SERIAL_SECTS
                for hc in range(NHC):
                    pw = PSW1[hc % 2]
                    for dc in range(NCH):
                        w = [exq, ew1, econv_prev[hc % 2], ev_w2_evac] if dc == 0 else ()
                        sch.op("tensor", "matmul", pw[:, :],
                               W1Q[:, dc * DH + hc * 128:dc * DH + (hc + 1) * 128],
                               XQ2[:, dc * TT:(dc + 1) * TT],
                               start=(dc == 0), stop=(dc == NCH - 1), waits=w)
                    epe = sch.ev("tensor")
                    hbuf = HCH[hc % 2]
                    sch.op("vector", "tensor_copy", hbuf[:, 0:2],
                           EDGE[:, 2 * hc:2 * hc + 2],
                           waits=[epe, eedge, ev_hch_used[hc % 2], ev_edge_used])
                    edsc = sch.op("vector", "tensor_tensor", hbuf[:, 2:TT + 2],
                                  pw[:, :], ABC3[:, 2 * TT:3 * TT], ALU.mult)
                    sch.op("vector", "tensor_copy", EDGE[:, 2 * hc:2 * hc + 2],
                           hbuf[:, TT:TT + 2], waits=[edsc])
                    stp = STT_[:, hc * TT:(hc + 1) * TT]
                    sch.op("scalar", "activation", stp, hbuf[:, 0:TT], AF.Identity,
                           bias=0.0, scale=CONVW[:, 3 * hc:3 * hc + 1],
                           waits=[edsc, ev_st_used, eam_prev_tile])
                    et1 = sch.ev("scalar")
                    sch.op("vector", "scalar_tensor_tensor", stp, hbuf[:, 1:TT + 1],
                           CONVW[:, 3 * hc + 1:3 * hc + 2], stp, ALU.mult, ALU.add,
                           waits=[et1])
                    econv = sch.op("vector", "scalar_tensor_tensor", stp,
                                   hbuf[:, 2:TT + 2],
                                   CONVW[:, 3 * hc + 2:3 * hc + 3], stp,
                                   ALU.mult, ALU.add)
                    econv_prev[hc % 2] = econv
                    ev_hch_used[hc % 2] = econv
                    esilu = sch.op("scalar", "activation", stp, stp, AF.Silu,
                                   bias=CONVB[:, hc:hc + 1], scale=1.0,
                                   waits=[econv])
                    if hc == 0:
                        sch.op("vector", "tensor_copy", AM2[:, :], stp,
                               waits=[esilu, eam_last])
                        eam_last = sch.op("vector", "tensor_copy", MN3[:, :], stp)
                    else:
                        sch.op("vector", "tensor_tensor", AM2[:, :],
                               AM2[:, :], stp, ALU.max, waits=[esilu])
                        eam_last = sch.op("vector", "tensor_tensor", MN3[:, :],
                                          MN3[:, :], stp, ALU.min)
                sch.serial = False
                ev_edge_used = sch.ev("vector")
                ev_xq_used_h[0] = sch.ev("tensor")
                esilu_all = esilu
                sch.serial = "rows" in SERIAL_SECTS
                for j in range(4):
                    sch.op("tensor", "transpose", PTR3[:, 0:128],
                           AM2[:, j * 128:(j + 1) * 128], IDF[:, :],
                           waits=[eam_last])
                    epe = sch.ev("tensor")
                    er = sch.vop("tensor_reduce", M2TM[:, j:j + 1],
                                 PTR3[:, 0:128], AX.X, ALU.max, waits=[epe])
                    sch.op("tensor", "nop", waits=[er], inc=0)
                    sch.op("tensor", "transpose", PTR3[:, 0:128],
                           MN3[:, j * 128:(j + 1) * 128], IDF[:, :],
                           waits=[eam_last, er])
                    epe = sch.ev("tensor")
                    er = sch.vop("tensor_reduce", MN3T[:, j:j + 1],
                                 PTR3[:, 0:128], AX.X, ALU.min, waits=[epe])
                    sch.op("tensor", "nop", waits=[er], inc=0)
                sch.vop("tensor_scalar", MN3T[:, :], MN3T[:, :], -1.0,
                        None, ALU.mult)
                sch.vop("tensor_tensor", M2TM[:, :], M2TM[:, :],
                        MN3T[:, :], ALU.max)
                sch.vop("tensor_scalar", M2TM[:, :], M2TM[:, :], 1e-5,
                        None, ALU.max)
                sch.vop("tensor_scalar", D2TM[:, :], M2TM[:, :],
                        INVW2[:, :], 1.0 / 127.0, ALU.mult, ALU.mult)
                sch.vop("reciprocal", M2TM[:, :], M2TM[:, :])
                sch.vop("tensor_scalar", M2TM[:, :], M2TM[:, :], 127.0,
                        None, ALU.mult)
                edve = sch.ev("vector")
                ebc3 = None
                for (tm, dst) in ((M2TM, A2BC), (D2TM, D2BC)):
                    for j in range(4):
                        sch.op("tensor", "matmul", PBC3[:, j * 128:(j + 1) * 128],
                               tm[:, j:j + 1].broadcast_to((128, 128)),
                               IDF[:, :], start=True, stop=True,
                               waits=[edve, ebc3] if j == 0 else ())
                    epe = sch.ev("tensor")
                    ebc3 = sch.op("scalar", "activation", dst[:, :], PBC3[:, :],
                                  AF.Identity, bias=0.0, scale=1.0, waits=[epe])
                sch.serial = False
                epe_w1_done = sch.ev("tensor")
                if ti + 1 < NTT:
                    state[ti + 1] = emit_bcq(ti + 1, [ebc3])
                    eabc_next = state[ti + 1][0]
                else:
                    eabc_next = None
                if dbg is not None and ti == 0:
                    edmp = sch.op("sync", "dma_start", dbg["stt"], STT_[:, :],
                                  waits=[esilu_all, ebc3], inc=16)
                    sch.op("sync", "dma_start", dbg["a2r"], A2BC[:, :],
                           waits=[ebc3], inc=16)
                    sch.op("sync", "dma_start", dbg["d2r"], D2BC[:, :],
                           waits=[ebc3], inc=16)
                    ev_st_used = sch.ev("gpsimd")
                sch.serial = "w2" in SERIAL_SECTS
                PSW2 = PK
                for hc in range(NHC):
                    sch.op("vector", "tensor_mul", TMQ[:, :],
                           STT_[:, hc * TT:(hc + 1) * TT], A2BC[:, :],
                           waits=[ebc3, ev_hq_used[hc % 2]])
                    sch.op("vector", "tensor_copy", QI3[:, :], TMQ[:, :],
                           waits=[sch.ev("vector")])
                    ehq = sch.op("vector", "tensor_scalar", HQ[hc % 2][:, :],
                                 QI3[:, :], 127.0, -128.0, ALU.min, ALU.max,
                                 waits=[sch.ev("vector")])
                    if dbg is not None:
                        ehq = sch.op("sync", "dma_start", dbg["hqa"][ti, hc, :, :],
                                     HQ[hc % 2][:, :], waits=[ehq], inc=16)
                    if dbg is not None and ti == 0 and hc == 0:
                        ehq = sch.op("sync", "dma_start", dbg["hq0"],
                                     HQ[0][:, :], waits=[ehq], inc=16)
                    ew2 = sch.op("sync", "dma_start", W2S[hc % 2][:, :],
                                 w2q_s[hc, :, :],
                                 waits=[ev_w2s[hc % 2], ev_wq_dma], inc=16)
                    for co in range(8):
                        w = ()
                        if co == 0:
                            w = [ehq, ew2]
                            if hc == 0:
                                w += [epe_w1_done, ev_w2_evac, eabc_next]
                        sch.op("tensor", "matmul", PSW2[co][:, :],
                               W2S[hc % 2][:, co * 128:(co + 1) * 128],
                               HQ[hc % 2][:, :],
                               start=(hc == 0), stop=(hc == NHC - 1), waits=w)
                    ev_hq_used[hc % 2] = sch.ev("tensor")
                    ev_w2s[hc % 2] = sch.ev("tensor")
                ev_st_used = sch.ev("vector")
                epe_w2 = sch.ev("tensor")
                if dbg is not None:
                    sch.op("sync", "dma_start", dbg["d2a"][ti, :, :], D2BC[:, :],
                           waits=[ebc3], inc=16)
                    sch.op("sync", "dma_start", dbg["a2a"][ti, :, :], A2BC[:, :],
                           waits=[ebc3], inc=16)
                    edmp = sch.op("sync", "dma_start", dbg["stta"][ti, :, :],
                                  STT_[:, :], waits=[esilu_all, eam_last], inc=16)
                    ev_st_used = [ev_st_used, edmp] if not isinstance(ev_st_used, list) else ev_st_used + [edmp]
                lo = t0 - 1
                for co in range(8):
                    if lo < 0:
                        edr = sch.op("sync", "dma_start", XR3[co % 2][:, 1:TT],
                                     xp_s[s, co, :, 0:TT - 1],
                                     waits=[ev_xr3[co % 2], ev_xp_last], inc=16)
                    else:
                        edr = sch.op("sync", "dma_start", XR3[co % 2][:, :],
                                     xp_s[s, co, :, lo:lo + TT],
                                     waits=[ev_xr3[co % 2], ev_xp_last], inc=16)
                    if dbg is not None:
                        edr = sch.op("sync", "dma_start", dbg["xra"][ti, co, :, :],
                                     XR3[co % 2][:, :], waits=[edr], inc=16)
                    sch.op("vector", "tensor_tensor", TMQ[:, :], PSW2[co][:, :],
                           D2BC[:, :], ALU.mult,
                           waits=[epe_w2, edr, ev_xoc[co % 2]])
                    ev_w2_evac = sch.op("vector", "tensor_tensor",
                                        XOC[co % 2][:, :], TMQ[:, :],
                                        XR3[co % 2][:, :], ALU.add,
                                        waits=[sch.ev("vector")])
                    ev_xr3[co % 2] = ev_w2_evac
                    if lo < 0:
                        ev_xoc[co % 2] = sch.op("sync", "dma_start",
                                                out[s, co * 128:(co + 1) * 128, 0:TT - 1],
                                                XOC[co % 2][:, 1:TT],
                                                waits=[ev_w2_evac], inc=16)
                    else:
                        ev_xoc[co % 2] = sch.op("sync", "dma_start",
                                                out[s, co * 128:(co + 1) * 128, lo:lo + TT],
                                                XOC[co % 2][:, :],
                                                waits=[ev_w2_evac], inc=16)

            sch.serial = False
            PSS = [PK[i][:, 0:1] for i in range(8)]
            PTR4 = PK[1]
            PBC4 = PK[2][:, 4:5]
            wcv = [ev_edge_used, sch.ev("vector")]
            for hc in range(NHC):
                sch.vop("tensor_scalar", SS[:, hc:hc + 1],
                        EDGE[:, 2 * hc:2 * hc + 1], CONVW[:, 3 * hc:3 * hc + 1],
                        None, ALU.mult, waits=wcv)
                sch.vop("scalar_tensor_tensor", SS[:, hc:hc + 1],
                        EDGE[:, 2 * hc + 1:2 * hc + 2],
                        CONVW[:, 3 * hc + 1:3 * hc + 2], SS[:, hc:hc + 1],
                        ALU.mult, ALU.add)
            edve = sch.vop("tensor_add", SS[:, :], SS[:, :], CONVB[:, :])
            esilu = sch.op("scalar", "activation", SS[:, :], SS[:, :], AF.Silu,
                           bias=0.0, scale=1.0, waits=[edve])
            er1 = sch.vop("tensor_reduce", TMQ[:, 0:1], SS[:, :], AX.X,
                          ALU.max, apply_absolute_value=True, waits=[esilu])
            sch.op("tensor", "transpose", PTR4[0:1, 0:128], TMQ[:, 0:1], IDF[:, :],
                   waits=[er1, epe_w2])
            epe = sch.ev("tensor")
            sch.vop("tensor_reduce", TMQ[0:1, 1:2], PTR4[0:1, 0:128],
                    AX.X, ALU.max, waits=[epe])
            sch.vop("tensor_scalar", TMQ[0:1, 1:2], TMQ[0:1, 1:2], 1e-5,
                    None, ALU.max)
            sch.vop("tensor_scalar", TMQ[0:1, 2:3], TMQ[0:1, 1:2],
                    INVW2[0:1, :], 1.0 / 127.0, ALU.mult, ALU.mult)
            sch.vop("reciprocal", TMQ[0:1, 3:4], TMQ[0:1, 1:2])
            sch.vop("tensor_scalar", TMQ[0:1, 3:4], TMQ[0:1, 3:4], 127.0,
                    None, ALU.mult)
            edve = sch.ev("vector")
            sch.op("tensor", "matmul", PBC4, ONE_R_F[:, :], TMQ[0:1, 3:4],
                   start=True, stop=True, waits=[edve])
            epe = sch.ev("tensor")
            ea2 = sch.op("scalar", "activation", TMQ[:, 4:5], PBC4, AF.Identity,
                         bias=0.0, scale=1.0, waits=[epe])
            sch.vop("tensor_scalar", SS[:, :], SS[:, :], TMQ[:, 4:5],
                    None, ALU.mult, waits=[ea2])
            sch.vop("tensor_copy", QI3[:, 0:NHC], SS[:, :])
            ehqs = sch.vop("tensor_scalar", HQS[:, :], QI3[:, 0:NHC],
                           127.0, -128.0, ALU.min, ALU.max)
            sch.op("tensor", "matmul", PBC4, ONE_R_F[:, :], TMQ[0:1, 2:3],
                   start=True, stop=True, waits=[ehqs])
            epe = sch.ev("tensor")
            ed2 = sch.op("scalar", "activation", TMQ[:, 5:6], PBC4, AF.Identity,
                         bias=0.0, scale=1.0, waits=[epe])
            exs = None
            for co in range(8):
                exs = sch.op("sync", "dma_start", XRS[:, co:co + 1],
                             xp_s[s, co, :, S - 1:S], waits=[ev_xp_last], inc=16,
                             allow_slow_non_contiguous=True)
            ew2p = [None, None]
            for hc in range(NHC):
                ew2 = sch.op("sync", "dma_start", W2S[hc % 2][:, :], w2q_s[hc, :, :],
                             waits=[ew2p[hc % 2], ev_w2s[hc % 2]], inc=16)
                ev_w2s[hc % 2] = None
                for co in range(8):
                    w = [ew2, ehqs, ed2] if co == 0 else ()
                    sch.op("tensor", "matmul", PSS[co],
                           W2S[hc % 2][:, co * 128:(co + 1) * 128],
                           HQS[:, hc:hc + 1],
                           start=(hc == 0), stop=(hc == NHC - 1), waits=w)
                ew2p[hc % 2] = sch.ev("tensor")
            epe = sch.ev("tensor")
            for co in range(8):
                sch.vop("tensor_scalar", XOS[:, co:co + 1], PSS[co],
                        TMQ[:, 5:6], None, ALU.mult, waits=[epe, ed2])
            exo = sch.vop("tensor_add", XOS[:, :], XOS[:, :], XRS[:, :],
                          waits=[exs])
            for co in range(8):
                sch.op("sync", "dma_start",
                       out[s, co * 128:(co + 1) * 128, S - 1:S], XOS[:, co:co + 1],
                       waits=[exo], inc=16, allow_slow_non_contiguous=True)
            sch.flush(blk)


def _host_inputs(L0_attn_x, L0_attn_w_qkv, L0_attn_b_qkv, L0_attn_qn_w,
                 L0_attn_kn_w, L0_attn_w_proj, L0_attn_b_proj,
                 L0_ffn_w1, L0_ffn_conv_w, L0_ffn_conv_b, L0_ffn_w2):
    C_ = np.ascontiguousarray
    shared = {
        "wqkvT": C_(np.asarray(L0_attn_w_qkv, np.float32).T),
        "wprojT": C_(np.asarray(L0_attn_w_proj, np.float32).T),
        "w1T": C_(np.asarray(L0_ffn_w1, np.float32).T),
        "w2T": C_(np.asarray(L0_ffn_w2, np.float32).T),
        "bqkv_col": C_(np.asarray(L0_attn_b_qkv, np.float32).reshape(24, 128).T),
        "bv_row": C_(np.asarray(L0_attn_b_qkv, np.float32)[2 * C:].reshape(1, C)),
        "bproj_col": C_(np.asarray(L0_attn_b_proj, np.float32).reshape(8, 128).T),
        "convw_col": C_(np.asarray(L0_ffn_conv_w, np.float32).reshape(NHC, 128, 3)
                        .transpose(1, 0, 2).reshape(128, 3 * NHC)),
        "convb_col": C_(np.asarray(L0_ffn_conv_b, np.float32).reshape(NHC, 128).T),
        "qn2": C_(np.tile(np.asarray(L0_attn_qn_w, np.float32), 2).reshape(128, 1)),
        "kn2": C_(np.tile(np.asarray(L0_attn_kn_w, np.float32), 2).reshape(128, 1)),
        "ident": np.eye(128, dtype=np.float32),
        "sel2": C_(np.stack([(np.arange(128) < 64).astype(np.float32),
                             (np.arange(128) >= 64).astype(np.float32)])),
        "blkones": C_(np.stack([(np.arange(128) < 64).astype(np.float32),
                                (np.arange(128) >= 64).astype(np.float32)]).T),
    }
    x = np.asarray(L0_attn_x, np.float32)
    in_maps = []
    for core in range(NCORE):
        m = dict(shared)
        xs = x[core * SEQS:(core + 1) * SEQS]
        m["x"] = C_(xs.transpose(0, 2, 1))
        in_maps.append(m)
    return in_maps


def kernel(**inputs):
    in_maps = _host_inputs(**inputs)
    nc = bass.Bass("TRN2", target_bir_lowering=False)
    build(nc, debug=0)
    res = run_bass_kernel_spmd(nc, in_maps, core_ids=list(range(NCORE)))
    outs = []
    for core in range(NCORE):
        o = np.asarray(res.results[core]["out"], np.float32)
        outs.append(o.transpose(0, 2, 1))
    return np.ascontiguousarray(np.concatenate(outs, axis=0))
